# version 01
# baseline (speedup 1.0000x reference)
import numpy as np
import ml_dtypes
import concourse.bass as bass
import concourse.mybir as mybir
from concourse.bass_utils import run_bass_kernel_spmd

F32 = mybir.dt.float32
BF16 = mybir.dt.bfloat16
AF = mybir.ActivationFunctionType
ALU = mybir.AluOpType
AX = mybir.AxisListType.X

D = 2048
S = 8192
B = 2
NT = 16
TLOC = NT * 128
NFC = D // 128
NTC = TLOC // 512
NH = 16
HD = 128
NCORE = 8
EPS = 1e-5

BF = ml_dtypes.bfloat16

ENGS = ("tensor", "vector", "scalar", "sync", "gpsimd")


class Sem:
    def __init__(self, h):
        self.h = h
        self.n = 0


class Prog:

    def __init__(self, nc):
        self.nc = nc
        self.ops = {e: [] for e in ENGS}
        self._ctxs = []
        self._bar = None
        self._barn = 0

    def sem(self, name):
        cm = self.nc.semaphore(name)
        h = cm.__enter__()
        self._ctxs.append(cm)
        return Sem(h)

    def _app(self, eng, fn):
        self.ops[eng].append(fn)

    def wait(self, eng, sem, val):
        if val is None:
            return
        self._app(eng, lambda e, s=sem, v=val: e.wait_ge(s.h, v))

    def op(self, eng, fn, sem=None, k=1):
        if sem is None:
            self._app(eng, fn)
            return None
        sem.n += k
        val = sem.n
        def g(e, fn=fn, s=sem, kk=k):
            fn(e).then_inc(s.h, kk)
        self._app(eng, g)
        return val

    def edged(self, eng, fn, sem):
        v = self.op(eng, fn, sem)
        self.wait(eng, sem, v)
        return v

    def barrier(self):
        if self._bar is None:
            self._bar = self.sem("bar")
        self._barn += 1
        target = len(ENGS) * self._barn
        for e in ENGS:
            def f(eng, t=target, s=self._bar):
                eng.drain()
                eng.sem_inc(s.h, 1)
                eng.wait_ge(s.h, t)
            self._app(e, f)
        self._bar.n = target

    def flush(self, blk):
        def run(e, lst):
            for f in lst:
                f(e)
        blk.tensor(lambda e: run(e, self.ops["tensor"]))
        blk.vector(lambda e: run(e, self.ops["vector"]))
        blk.scalar(lambda e: run(e, self.ops["scalar"]))
        blk.sync(lambda e: run(e, self.ops["sync"]))
        blk.gpsimd(lambda e: run(e, self.ops["gpsimd"]))


class Bank:
    def __init__(self, tile):
        self.tile = tile
        self.free_sem = None
        self.free_val = None

    def wait_free(self, p):
        if self.free_sem is not None:
            p.wait("tensor", self.free_sem, self.free_val)

    def set_free(self, sem, val):
        self.free_sem = sem
        self.free_val = val



def emit_consts(p, gc, ones, ident, tri=None, eps_t=None):
    p.op("gpsimd", lambda g: g.memset(ones[:, :], 1.0))
    p.op("gpsimd", lambda g: g.memset(ident[:, :], 0.0))
    p.op("gpsimd", lambda g: g.affine_select(
        out=ident[:, :], in_=ident[:, :], compare_op=ALU.not_equal, fill=1.0,
        base=0, pattern=[[-1, 128]], channel_multiplier=1))
    if tri is not None:
        p.op("gpsimd", lambda g: g.memset(tri[:, :], 1.0))
        p.op("gpsimd", lambda g: g.affine_select(
            out=tri[:, :], in_=tri[:, :], compare_op=ALU.is_ge, fill=0.0,
            base=0, pattern=[[-1, 128]], channel_multiplier=1))
    if eps_t is not None:
        p.op("gpsimd", lambda g: g.memset(eps_t[:, :], EPS))
    return p.op("gpsimd", lambda g: g.engine_nop(), gc)


def emit_ln_stats(p, sems, src, ntok, ones, mb, rb, bankA, bankB, sq, eps_t,
                  extra_tensor_wait=None):
    mm, sc, vx = sems["mm"], sems["sc"], sems["vx"]
    ntc = ntok // 512
    sq_free = [None, None]
    for tc in range(ntc):
        t0 = tc * 512
        bankA.wait_free(p)
        bankB.wait_free(p)
        mmv = None
        for fc in range(NFC):
            par = fc % 2
            if sq_free[par] is not None:
                p.wait("scalar", mm, sq_free[par])
            v_sq = p.op("scalar", lambda s, fc=fc, t0=t0, par=par: s.activation(
                sq[:, par, :], src(fc)[:, t0:t0 + 512], AF.Square), sc)
            if extra_tensor_wait is not None and fc == 0 and tc == 0:
                p.wait("tensor", *extra_tensor_wait)
            p.op("tensor", lambda t, fc=fc, t0=t0: t.matmul(
                bankA.tile[:, :], ones[:, :], src(fc)[:, t0:t0 + 512],
                start=(fc == 0), stop=(fc == NFC - 1)))
            p.wait("tensor", sc, v_sq)
            mmv = p.op("tensor", lambda t, par=par, fc=fc: t.matmul(
                bankB.tile[:, :], ones[:, :], sq[:, par, :],
                start=(fc == 0), stop=(fc == NFC - 1)), mm)
            sq_free[par] = mmv
        p.wait("vector", mm, mmv)
        v1 = p.edged("vector", lambda v, t0=t0: v.tensor_scalar_mul(
            mb[:, t0:t0 + 512], bankA.tile[:, :], 1.0 / D), vx)
        p.edged("vector", lambda v, t0=t0: v.scalar_tensor_tensor(
            rb[:, t0:t0 + 512], mb[:, t0:t0 + 512], -1.0, mb[:, t0:t0 + 512],
            ALU.mult, ALU.mult), vx)
        v3 = p.edged("vector", lambda v, t0=t0: v.scalar_tensor_tensor(
            rb[:, t0:t0 + 512], bankB.tile[:, :], 1.0 / D, rb[:, t0:t0 + 512],
            ALU.mult, ALU.add), vx)
        bankA.set_free(vx, v1)
        bankB.set_free(vx, v3)
        p.wait("scalar", vx, v3)
        s1 = p.edged("scalar", lambda s, t0=t0: s.activation(
            rb[:, t0:t0 + 512], rb[:, t0:t0 + 512], AF.Ln, bias=eps_t[:, 0:1]), sc)
        s2 = p.edged("scalar", lambda s, t0=t0: s.activation(
            rb[:, t0:t0 + 512], rb[:, t0:t0 + 512], AF.Exp, scale=-0.5), sc)
    return sc.n


def emit_normalize_inplace(p, sems, xln, mb, rb, tmp, stats_ready,
                           consumers_done):
    vx, sc = sems["vx"], sems["sc"]
    p.wait("vector", sems["sc"], stats_ready)
    for sem, val in consumers_done:
        p.wait("vector", sem, val)
    last = None
    for fc in range(NFC):
        for tc in range(NTC):
            t0 = tc * 512
            p.edged("vector", lambda v, fc=fc, t0=t0: v.tensor_tensor(
                tmp[:, 0:512], xln[:, fc, t0:t0 + 512], mb[:, t0:t0 + 512],
                ALU.subtract), vx)
            last = p.edged("vector", lambda v, fc=fc, t0=t0: v.tensor_tensor(
                xln[:, fc, t0:t0 + 512], tmp[:, 0:512], rb[:, t0:t0 + 512],
                ALU.mult), vx)
    return last



INTER = 5632
NIC = INTER // 128
FFN_GRP = 4


def build_ffn0():
    nc = bass.Bass("TRN2", target_bir_lowering=False, debug=False)
    x1 = nc.dram_tensor("x1", [D, TLOC], BF16, kind="ExternalInput").ap()
    gate_w = nc.dram_tensor("gate_w", [INTER, D], F32, kind="ExternalInput").ap()
    up_w = nc.dram_tensor("up_w", [INTER, D], F32, kind="ExternalInput").ap()
    down_w = nc.dram_tensor("down_w", [D, INTER], F32, kind="ExternalInput").ap()
    gate_s = nc.dram_tensor("gate_s", [128, NIC], F32, kind="ExternalInput").ap()
    up_s = nc.dram_tensor("up_s", [128, NIC], F32, kind="ExternalInput").ap()
    down_s = nc.dram_tensor("down_s", [128, NFC], F32, kind="ExternalInput").ap()
    x2 = nc.dram_tensor("x2", [D, TLOC], BF16, kind="ExternalOutput").ap()

    import contextlib
    with contextlib.ExitStack() as _st:
        ones = _st.enter_context(nc.sbuf_tensor("ones", [128, 128], BF16))
        ident = _st.enter_context(nc.sbuf_tensor("ident", [128, 128], BF16))
        eps_t = _st.enter_context(nc.sbuf_tensor("eps_t", [128, 1], F32))
        xln = _st.enter_context(nc.sbuf_tensor("xln", [128, NFC, TLOC], BF16))
        acc = _st.enter_context(nc.sbuf_tensor("acc", [128, NFC, TLOC], BF16))
        mb = _st.enter_context(nc.sbuf_tensor("mb", [128, TLOC], F32))
        rb = _st.enter_context(nc.sbuf_tensor("rb", [128, TLOC], F32))
        sq = _st.enter_context(nc.sbuf_tensor("sq", [128, 2, 512], BF16))
        gnat = _st.enter_context(nc.sbuf_tensor("gnat", [128, 2, D], BF16))
        unat = _st.enter_context(nc.sbuf_tensor("unat", [128, 2, D], BF16))
        gT = _st.enter_context(nc.sbuf_tensor("gT", [128, D], BF16))
        uT = _st.enter_context(nc.sbuf_tensor("uT", [128, D], BF16))
        gact = _st.enter_context(nc.sbuf_tensor("gact", [128, TLOC], BF16))
        uact = _st.enter_context(nc.sbuf_tensor("uact", [128, TLOC], BF16))
        hdn = _st.enter_context(nc.sbuf_tensor("hdn", [128, FFN_GRP, TLOC], BF16))
        dnat = _st.enter_context(nc.sbuf_tensor("dnat", [128, 2, FFN_GRP * 128], BF16))
        dT = _st.enter_context(nc.sbuf_tensor("dT", [128, FFN_GRP, 128], BF16))
        gs_sb = _st.enter_context(nc.sbuf_tensor("gs_sb", [128, NIC], F32))
        us_sb = _st.enter_context(nc.sbuf_tensor("us_sb", [128, NIC], F32))
        ds_sb = _st.enter_context(nc.sbuf_tensor("ds_sb", [128, NFC], F32))
        tmp = _st.enter_context(nc.sbuf_tensor("tmp", [128, 2, 512], BF16))
        xres = _st.enter_context(nc.sbuf_tensor("xres", [128, 2, 512], BF16))
        pg0 = _st.enter_context(nc.psum_tensor("pg0", [128, 512], F32))
        pg1 = _st.enter_context(nc.psum_tensor("pg1", [128, 512], F32))
        pu0 = _st.enter_context(nc.psum_tensor("pu0", [128, 512], F32))
        pu1 = _st.enter_context(nc.psum_tensor("pu1", [128, 512], F32))
        pd0 = _st.enter_context(nc.psum_tensor("pd0", [128, 512], F32))
        pd1 = _st.enter_context(nc.psum_tensor("pd1", [128, 512], F32))
        ptr0 = _st.enter_context(nc.psum_tensor("ptr0", [128, 1024], BF16))
        ptr1 = _st.enter_context(nc.psum_tensor("ptr1", [128, 1024], BF16))
        blk = _st.enter_context(nc.Block())
        p = Prog(nc)
        ld = p.sem("s_ld")
        gld = p.sem("s_gld")
        mm = p.sem("s_mm")
        sc = p.sem("s_sc")
        vx = p.sem("s_vx")
        st = p.sem("s_st")
        gc = p.sem("s_gc")
        sems = dict(mm=mm, sc=sc, vx=vx)

        bA = Bank(pg0); bB = Bank(pg1)
        bU0 = Bank(pu0); bU1 = Bank(pu1)
        bD0 = Bank(pd0); bD1 = Bank(pd1)
        bT0 = Bank(ptr0); bT1 = Bank(ptr1)

        consts_done = emit_consts(p, gc, ones, ident, eps_t=eps_t)

        for fc in range(NFC):
            p.op("sync", lambda sy, fc=fc: sy.dma_start(
                out=xln[:, fc, :], in_=x1[fc * 128:(fc + 1) * 128, :]), ld, 16)
        p.op("sync", lambda sy: sy.dma_start(out=gs_sb[:, :], in_=gate_s), ld, 16)
        p.op("sync", lambda sy: sy.dma_start(out=us_sb[:, :], in_=up_s), ld, 16)
        x1_ld = p.op("sync", lambda sy: sy.dma_start(out=ds_sb[:, :], in_=down_s), ld, 16)

        p.wait("scalar", ld, 16 * NFC)
        p.wait("tensor", ld, 16 * NFC)
        stats_ready = emit_ln_stats(
            p, sems, lambda fc: xln[:, fc, :], TLOC, ones, mb, rb, bA, bB,
            sq, eps_t, extra_tensor_wait=(gc, consts_done))
        stats_mm = mm.n
        stats_sc = sc.n

        xln_ready = emit_normalize_inplace(
            p, sems, xln, mb, rb, tmp[:, 0, :], stats_ready,
            [(mm, stats_mm), (sc, stats_sc)])

        def emit_strip_load(ic):
            par = ic % 2
            vals = []
            p.wait("gpsimd", gc, consts_done)
            if strip_free[par] is not None:
                p.wait("gpsimd", mm, strip_free[par])
            vals.append(p.op("gpsimd", lambda g, ic=ic, par=par: g.dma_start(
                out=gnat[:, par, :], in_=gate_w[ic * 128:(ic + 1) * 128, :]), gld, 16))
            vals.append(p.op("gpsimd", lambda g, ic=ic, par=par: g.dma_start(
                out=unat[:, par, :], in_=up_w[ic * 128:(ic + 1) * 128, :]), gld, 16))
            strip_ld[ic] = vals[-1]

        strip_ld = {}
        strip_free = [None, None]
        NGRP = (NIC + FFN_GRP - 1) // FFN_GRP

        emit_strip_load(0)
        emit_strip_load(1)

        gact_free = [[None] * NTC, [None] * NTC]
        hdn_free = [None] * FFN_GRP
        dnat_free = [None, None]
        tmp_free = [None, None]
        xres_free = [None, None]
        gu_mm_done = [None]

        for ic in range(NIC):
            par = ic % 2
            icg = ic % FFN_GRP
            grp_i = ic // FFN_GRP
            p.wait("tensor", gld, strip_ld[ic])

            tr_last = None
            for hc in range(NFC):
                for (nat, T, bT) in ((gnat, gT, (bT0 if hc % 2 == 0 else bT1)),
                                     (unat, uT, (bT1 if hc % 2 == 0 else bT0))):
                    bT.wait_free(p)
                    tv = p.op("tensor", lambda t, nat=nat, par=par, hc=hc, bT=bT: t.transpose(
                        bT.tile[:, 0:128], nat[:, par, hc * 128:(hc + 1) * 128],
                        ident[:, :]), mm)
                    p.wait("vector", mm, tv)
                    cv = p.op("vector", lambda v, T=T, hc=hc, bT=bT: v.tensor_copy(
                        T[:, hc * 128:(hc + 1) * 128], bT.tile[:, 0:128]), vx)
                    bT.set_free(vx, cv)
                    tr_last = tv
            strip_free[par] = tr_last
            trT_done = vx.n
            if ic + 2 < NIC:
                emit_strip_load(ic + 2)

            p.wait("tensor", vx, trT_done)
            if ic == 0:
                p.wait("tensor", vx, xln_ready)
            gu_mm = []
            for tc in range(NTC):
                t0 = tc * 512
                for which, (T, bank0, bank1, act, s_sb, dst) in enumerate((
                        (gT, bA, bB, AF.Silu, gs_sb, gact),
                        (uT, bU0, bU1, AF.Identity, us_sb, uact))):
                    bank = bank0 if tc % 2 == 0 else bank1
                    bank.wait_free(p)
                    if gact_free[which][tc] is not None:
                        p.wait("scalar", vx, gact_free[which][tc])
                    for hc in range(NFC):
                        mv = p.op("tensor", lambda t, T=T, hc=hc, t0=t0, bank=bank: t.matmul(
                            bank.tile[:, :], T[:, hc * 128:(hc + 1) * 128],
                            xln[:, hc, t0:t0 + 512],
                            start=(hc == 0), stop=(hc == NFC - 1)),
                            mm if hc == NFC - 1 else None)
                    p.wait("scalar", mm, mv)
                    ev = p.op("scalar", lambda s, dst=dst, t0=t0, bank=bank, act=act, s_sb=s_sb, ic=ic: s.activation(
                        dst[:, t0:t0 + 512], bank.tile[:, :], act,
                        scale=s_sb[:, ic:ic + 1]), sc)
                    bank.set_free(sc, ev)
                    gu_mm.append(mv)
                p.wait("vector", sc, sc.n)
                if hdn_free[icg] is not None:
                    p.wait("vector", mm, hdn_free[icg])
                tt = p.edged("vector", lambda v, icg=icg, t0=t0: v.tensor_tensor(
                    hdn[:, icg, t0:t0 + 512], gact[:, t0:t0 + 512],
                    uact[:, t0:t0 + 512], ALU.mult), vx)
                gact_free[0][tc] = tt
                gact_free[1][tc] = tt
            gu_mm_done[0] = gu_mm[-1]
            hdn_ready_v = vx.n

            if icg == FFN_GRP - 1 or ic == NIC - 1:
                gsz = icg + 1
                g0 = grp_i * FFN_GRP
                for hc in range(NFC):
                    dpar = hc % 2
                    if dnat_free[dpar] is not None:
                        p.wait("gpsimd", mm, dnat_free[dpar])
                    dl = p.op("gpsimd", lambda g, hc=hc, g0=g0, gsz=gsz, dpar=dpar: g.dma_start(
                        out=dnat[:, dpar, 0:gsz * 128],
                        in_=down_w[hc * 128:(hc + 1) * 128, g0 * 128:(g0 + gsz) * 128]), gld, 16)
                    p.wait("tensor", gld, dl)
                    dtr = None
                    for k in range(gsz):
                        bT = bT0 if k % 2 == 0 else bT1
                        bT.wait_free(p)
                        tv = p.op("tensor", lambda t, k=k, dpar=dpar, bT=bT: t.transpose(
                            bT.tile[:, 0:128], dnat[:, dpar, k * 128:(k + 1) * 128],
                            ident[:, :]), mm)
                        p.wait("vector", mm, tv)
                        cv = p.op("vector", lambda v, k=k, bT=bT: v.tensor_copy(
                            dT[:, k, :], bT.tile[:, 0:128]), vx)
                        bT.set_free(vx, cv)
                        dtr = tv
                    dnat_free[dpar] = dtr
                    dT_done = vx.n
                    p.wait("tensor", vx, dT_done)
                    p.wait("tensor", vx, hdn_ready_v)
                    for tc in range(NTC):
                        t0 = tc * 512
                        bank = bD0 if tc % 2 == 0 else bD1
                        bank.wait_free(p)
                        for k in range(gsz):
                            mv = p.op("tensor", lambda t, k=k, t0=t0, bank=bank, gsz=gsz: t.matmul(
                                bank.tile[:, :], dT[:, k, :], hdn[:, k, t0:t0 + 512],
                                start=(k == 0), stop=(k == gsz - 1)),
                                mm if k == gsz - 1 else None)
                        p.wait("scalar", mm, mv)
                        if grp_i == 0:
                            ev = p.op("scalar", lambda s, hc=hc, t0=t0, bank=bank: s.activation(
                                acc[:, hc, t0:t0 + 512], bank.tile[:, :],
                                AF.Identity, scale=ds_sb[:, hc:hc + 1]), sc)
                            bank.set_free(sc, ev)
                        else:
                            tpar = (hc * NTC + tc) % 2
                            if tmp_free[tpar] is not None:
                                p.wait("scalar", vx, tmp_free[tpar])
                            ev = p.op("scalar", lambda s, t0=t0, bank=bank, hc=hc, tpar=tpar: s.activation(
                                tmp[:, tpar, :], bank.tile[:, :],
                                AF.Identity, scale=ds_sb[:, hc:hc + 1]), sc)
                            bank.set_free(sc, ev)
                            p.wait("vector", sc, ev)
                            av = p.edged("vector", lambda v, hc=hc, t0=t0, tpar=tpar: v.tensor_tensor(
                                acc[:, hc, t0:t0 + 512], acc[:, hc, t0:t0 + 512],
                                tmp[:, tpar, :], ALU.add), vx)
                            tmp_free[tpar] = av
                for k in range(gsz):
                    hdn_free[k] = mm.n

        p.barrier()
        bA.free_sem = bB.free_sem = None
        stats2_ready = emit_ln_stats(
            p, sems, lambda fc: acc[:, fc, :], TLOC, ones, mb, rb, bA, bB,
            sq, eps_t)
        p.wait("vector", sc, stats2_ready)
        for fc in range(NFC):
            for tc in range(NTC):
                t0 = tc * 512
                par = (fc * NTC + tc) % 2
                if xres_free[par] is not None:
                    p.wait("sync", vx, xres_free[par])
                lv = p.op("sync", lambda sy, fc=fc, t0=t0, par=par: sy.dma_start(
                    out=xres[:, par, :], in_=x1[fc * 128:fc * 128 + 128, t0:t0 + 512]), ld, 16)
                p.wait("vector", ld, lv)
                p.edged("vector", lambda v, fc=fc, t0=t0: v.tensor_tensor(
                    tmp[:, 0, :], acc[:, fc, t0:t0 + 512], mb[:, t0:t0 + 512],
                    ALU.subtract), vx)
                p.edged("vector", lambda v, t0=t0: v.tensor_tensor(
                    tmp[:, 0, :], tmp[:, 0, :], rb[:, t0:t0 + 512], ALU.mult), vx)
                ov = p.edged("vector", lambda v, par=par, t0=t0: v.tensor_tensor(
                    tmp[:, 1, :], tmp[:, 0, :], xres[:, par, :], ALU.add), vx)
                xres_free[par] = ov
                p.wait("sync", vx, ov)
                sv = p.op("sync", lambda sy, fc=fc, t0=t0: sy.dma_start(
                    out=x2[fc * 128:fc * 128 + 128, t0:t0 + 512], in_=tmp[:, 1, :]), st, 16)
                p.wait("vector", st, sv)
        p.wait("sync", st, st.n)

        p.flush(blk)
    return nc


def kernel(**inputs):
    raise NotImplementedError("under construction")



def emit_featproj(p, sems, banks, xln, w_tile, nfc_out, stage, evac, xln_ready):
    mm = sems["mm"]
    p.wait("tensor", sems["vx"], xln_ready)
    for fc in range(nfc_out):
        for tc in range(NTC):
            t0 = tc * 512
            bank = banks[tc % 2]
            bank.wait_free(p)
            for hc in range(NFC):
                mv = p.op("tensor", lambda t, hc=hc, fc=fc, t0=t0, bank=bank: t.matmul(
                    bank.tile[:, :], w_tile(hc, fc), xln[:, hc, t0:t0 + 512],
                    start=(hc == 0), stop=(hc == NFC - 1)),
                    mm if hc == NFC - 1 else None)
            evac(fc, tc, bank, mv)


def emit_tokproj(p, sems, banks, xln, w_chunk, nfq_out, evac, xln_ready):
    mm = sems["mm"]
    p.wait("tensor", sems["vx"], xln_ready)
    for tt in range(NT):
        for fq in range(nfq_out):
            bank = banks[fq % 2]
            bank.wait_free(p)
            for hc in range(NFC):
                mv = p.op("tensor", lambda t, hc=hc, tt=tt, fq=fq, bank=bank: t.matmul(
                    bank.tile[:, :], xln[:, hc, tt * 128:(tt + 1) * 128],
                    w_chunk(hc, fq), start=(hc == 0), stop=(hc == NFC - 1)),
                    mm if hc == NFC - 1 else None)
            evac(tt, fq, bank, mv)



def build_qkv0(debug_wexp=False):
    nc = bass.Bass("TRN2", target_bir_lowering=False, debug=False)
    x0 = nc.dram_tensor("x0", [D, TLOC], F32, kind="ExternalInput").ap()
    w_codes = {}
    w_scales = {}
    for w in ("wq", "wk", "wv"):
        w_codes[w] = nc.dram_tensor(w + "_codes", [D, D], F32, kind="ExternalInput").ap()
        w_scales[w] = nc.dram_tensor(w + "_scales", [64, D], F32, kind="ExternalInput").ap()
    bq = nc.dram_tensor("bq", [128, NFC], F32, kind="ExternalInput").ap()
    bk = nc.dram_tensor("bk", [128, NFC], F32, kind="ExternalInput").ap()
    bv_row = nc.dram_tensor("bv_row", [1, D], F32, kind="ExternalInput").ap()
    xres_o = nc.dram_tensor("xres", [D, TLOC], BF16, kind="ExternalOutput").ap()
    qT_o = nc.dram_tensor("qT", [D, TLOC], BF16, kind="ExternalOutput").ap()
    kT_o = nc.dram_tensor("kT", [D, TLOC], BF16, kind="ExternalOutput").ap()
    vloc_o = nc.dram_tensor("vloc", [NT, NFC, 128, 128], BF16, kind="ExternalOutput").ap()
    wdump_o = nc.dram_tensor("wdump", [D, D], BF16, kind="ExternalOutput").ap() if debug_wexp else None
    xdump_o = nc.dram_tensor("xdump", [D, TLOC], BF16, kind="ExternalOutput").ap() if debug_wexp else None

    import contextlib
    with contextlib.ExitStack() as _st:
        ones = _st.enter_context(nc.sbuf_tensor("ones", [128, 128], BF16))
        ident = _st.enter_context(nc.sbuf_tensor("ident", [128, 128], BF16))
        eps_t = _st.enter_context(nc.sbuf_tensor("eps_t", [128, 1], F32))
        bsel = _st.enter_context(nc.sbuf_tensor("bsel", [128, 64, 32], BF16))
        xln = _st.enter_context(nc.sbuf_tensor("xln", [128, NFC, TLOC], BF16))
        wexp = _st.enter_context(nc.sbuf_tensor("wexp", [128, NFC, D], BF16))
        mb = _st.enter_context(nc.sbuf_tensor("mb", [128, TLOC], F32))
        rb = _st.enter_context(nc.sbuf_tensor("rb", [128, TLOC], F32))
        sq = _st.enter_context(nc.sbuf_tensor("sq", [128, 2, 512], BF16))
        codes = _st.enter_context(nc.sbuf_tensor("codes", [128, 2, D], F32))
        scal = _st.enter_context(nc.sbuf_tensor("scal", [128, D], BF16))
        bq_sb = _st.enter_context(nc.sbuf_tensor("bq_sb", [128, NFC], F32))
        bk_sb = _st.enter_context(nc.sbuf_tensor("bk_sb", [128, NFC], F32))
        bvr = _st.enter_context(nc.sbuf_tensor("bvr", [1, D], BF16))
        bvb = _st.enter_context(nc.sbuf_tensor("bvb", [128, D], F32))
        stage = _st.enter_context(nc.sbuf_tensor("stage", [128, 2, 512], BF16))
        tmp = _st.enter_context(nc.sbuf_tensor("tmp", [128, 2, 512], BF16))
        pA = _st.enter_context(nc.psum_tensor("pA", [128, 512], F32))
        pB = _st.enter_context(nc.psum_tensor("pB", [128, 512], F32))
        pe0 = _st.enter_context(nc.psum_tensor("pe0", [128, 512], F32))
        pe1 = _st.enter_context(nc.psum_tensor("pe1", [128, 512], F32))
        pp0 = _st.enter_context(nc.psum_tensor("pp0", [128, 512], F32))
        pp1 = _st.enter_context(nc.psum_tensor("pp1", [128, 512], F32))
        blk = _st.enter_context(nc.Block())

        p = Prog(nc)
        ld = p.sem("s_ld"); gld = p.sem("s_gld"); mm = p.sem("s_mm")
        sc = p.sem("s_sc"); vx = p.sem("s_vx"); st = p.sem("s_st")
        gc = p.sem("s_gc")
        sems = dict(mm=mm, sc=sc, vx=vx)
        bA = Bank(pA); bB = Bank(pB)
        bE = [Bank(pe0), Bank(pe1)]
        bP = [Bank(pp0), Bank(pp1)]

        consts_done = emit_consts(p, gc, ones, ident, eps_t=eps_t)
        p.op("gpsimd", lambda g: g.memset(scal[:, :], 0.0))
        p.op("gpsimd", lambda g: g.memset(bsel[:, :, :], 1.0))
        bsel_done = p.op("gpsimd", lambda g: g.affine_select(
            out=bsel[:, :, :], in_=bsel[:, :, :], compare_op=ALU.is_equal,
            fill=0.0, base=0, pattern=[[1, 64], [0, 32]],
            channel_multiplier=-1), gc)

        for fc in range(NFC):
            lv = p.op("gpsimd", lambda g, fc=fc: g.dma_start(
                out=xln[:, fc, :], in_=x0[fc * 128:(fc + 1) * 128, :]), gld, 16)
            p.wait("sync", gld, lv)
            p.op("sync", lambda sy, fc=fc: sy.dma_start(
                out=xres_o[fc * 128:(fc + 1) * 128, :], in_=xln[:, fc, :]), st, 16)
        p.op("sync", lambda sy: sy.dma_start(out=bq_sb[:, :], in_=bq), ld, 16)
        p.op("sync", lambda sy: sy.dma_start(out=bk_sb[:, :], in_=bk), ld, 16)
        bld = p.op("gpsimd", lambda g: g.dma_start(out=bvr[:, :], in_=bv_row), gld, 16)
        p.wait("tensor", gld, bld)
        p.wait("tensor", gc, consts_done)
        for c4 in range(4):
            bA.wait_free(p)
            mv = p.op("tensor", lambda t, c4=c4: t.matmul(
                pA[:, :], ones[0:1, 0:128], bvr[0:1, c4 * 512:(c4 + 1) * 512],
                start=True, stop=True), mm)
            p.wait("vector", mm, mv)
            cv = p.edged("vector", lambda v, c4=c4: v.tensor_copy(
                bvb[:, c4 * 512:(c4 + 1) * 512], pA[:, :]), vx)
            bA.set_free(vx, cv)

        p.wait("scalar", gld, 16 * NFC)
        p.wait("tensor", gld, 16 * NFC)
        stats_ready = emit_ln_stats(
            p, sems, lambda fc: xln[:, fc, :], TLOC, ones, mb, rb, bA, bB,
            sq, eps_t)
        stats_mm = mm.n
        stats_sc = sc.n
        xln_ready = emit_normalize_inplace(
            p, sems, xln, mb, rb, tmp[:, 0, :], stats_ready,
            [(mm, stats_mm), (sc, stats_sc)])

        def emit_expand(wname, scales_ap, codes_ap):
            p.wait("vector", mm, mm.n)
            p.wait("gpsimd", mm, mm.n)
            sv = p.op("gpsimd", lambda g: g.dma_start(out=scal[0:64, :], in_=scales_ap), gld, 16)
            cl = {}
            cfree = [vx.n, vx.n]
            def load_codes(hc):
                par = hc % 2
                if cfree[par] is not None:
                    p.wait("sync", vx, cfree[par])
                cl[hc] = p.op("sync", lambda sy, hc=hc, par=par: sy.dma_start(
                    out=codes[:, par, :], in_=codes_ap[hc * 128:(hc + 1) * 128, :]), ld, 16)
            load_codes(0); load_codes(1)
            wv = None
            for hc in range(NFC):
                par = hc % 2
                p.wait("tensor", ld, cl[hc])
                p.wait("tensor", gld, sv)
                p.wait("vector", ld, cl[hc])
                p.wait("tensor", gc, bsel_done)
                for c4 in range(4):
                    bank = bE[c4 % 2]
                    bank.wait_free(p)
                    mv = p.op("tensor", lambda t, hc=hc, c4=c4, bank=bank: t.matmul(
                        bank.tile[:, :], bsel[:, hc * 4:(hc + 1) * 4, :].rearrange("p a b -> p (a b)"),
                        scal[:, c4 * 512:(c4 + 1) * 512], start=True, stop=True), mm)
                    p.wait("vector", mm, mv)
                    wv = p.edged("vector", lambda v, hc=hc, c4=c4, par=par, bank=bank: v.tensor_tensor(
                        wexp[:, hc, c4 * 512:(c4 + 1) * 512],
                        codes[:, par, c4 * 512:(c4 + 1) * 512],
                        bank.tile[:, :], ALU.mult), vx)
                    bank.set_free(vx, wv)
                cfree[par] = wv
                if hc + 2 < NFC:
                    load_codes(hc + 2)
            return wv

        def qk_evac(out_ap, bias_sb):
            def evac(fc, tc, bank, mv):
                t0 = tc * 512
                spar = (fc * NTC + tc) % 2
                p.wait("scalar", mm, mv)
                if stage_free[spar] is not None:
                    p.wait("scalar", st, stage_free[spar])
                ev = p.op("scalar", lambda s, fc=fc, t0=t0, bank=bank, spar=spar: s.activation(
                    stage[:, spar, :], bank.tile[:, :], AF.Identity,
                    bias=bias_sb[:, fc:fc + 1]), sc)
                bank.set_free(sc, ev)
                p.wait("sync", sc, ev)
                dv = p.op("sync", lambda sy, fc=fc, t0=t0, spar=spar: sy.dma_start(
                    out=out_ap[fc * 128:(fc + 1) * 128, t0:t0 + 512],
                    in_=stage[:, spar, :]), st, 16)
                stage_free[spar] = dv
            return evac

        qk_list = (("wq", qT_o, bq_sb),) if debug_wexp else (("wq", qT_o, bq_sb), ("wk", kT_o, bk_sb))
        for wname, outap, bias_sb in qk_list:
            wv = emit_expand(wname, w_scales[wname], w_codes[wname])
            if debug_wexp and wname == "wq":
                p.wait("sync", vx, wv)
                p.wait("sync", vx, xln_ready)
                for hc in range(NFC):
                    p.op("sync", lambda sy, hc=hc: sy.dma_start(
                        out=wdump_o[hc * 128:(hc + 1) * 128, :], in_=wexp[:, hc, :]), st, 16)
                for hc in range(NFC):
                    p.op("sync", lambda sy, hc=hc: sy.dma_start(
                        out=xdump_o[hc * 128:(hc + 1) * 128, :], in_=xln[:, hc, :]), st, 16)
            stage_free = [None, None]
            p.wait("tensor", vx, wv)
            emit_featproj(p, sems, bP, xln,
                          lambda hc, fc: wexp[:, hc, fc * 128:(fc + 1) * 128],
                          NFC, stage, qk_evac(outap, bias_sb), xln_ready)

        if debug_wexp:
            p.wait("sync", st, st.n)
            p.flush(blk)
            return nc
        wv = emit_expand("wv", w_scales["wv"], w_codes["wv"])
        stage_free = [None, None]
        p.wait("tensor", vx, wv)

        def v_evac(tt, fq, bank, mv):
            spar = (tt * 4 + fq) % 2
            p.wait("vector", mm, mv)
            if stage_free[spar] is not None:
                p.wait("vector", st, stage_free[spar])
            ev = p.edged("vector", lambda v, fq=fq, bank=bank, spar=spar: v.tensor_tensor(
                stage[:, spar, :], bank.tile[:, :],
                bvb[:, fq * 512:(fq + 1) * 512], ALU.add), vx)
            bank.set_free(vx, ev)
            p.wait("sync", vx, ev)
            dv = None
            for k in range(4):
                fc = fq * 4 + k
                dv = p.op("sync", lambda sy, tt=tt, fc=fc, k=k, spar=spar: sy.dma_start(
                    out=vloc_o[tt, fc, :, :], in_=stage[:, spar, k * 128:(k + 1) * 128]), st, 16)
            stage_free[spar] = dv

        emit_tokproj(p, sems, bP, xln,
                     lambda hc, fq: wexp[:, hc, fq * 512:(fq + 1) * 512],
                     4, v_evac, xln_ready)

        p.wait("sync", st, st.n)
        p.flush(blk)
    return nc



RSQ = 0.08838834764831845


def build_attn0():
    nc = bass.Bass("TRN2", target_bir_lowering=False, debug=False)
    qT = nc.dram_tensor("qT", [D, TLOC], BF16, kind="ExternalInput").ap()
    kfull = nc.dram_tensor("kfull", [4, D, TLOC], BF16, kind="ExternalInput").ap()
    vfull = nc.dram_tensor("vfull", [4, NT, NFC, 128, 128], BF16, kind="ExternalInput").ap()
    aoT = nc.dram_tensor("aoT", [D, TLOC], BF16, kind="ExternalOutput").ap()
    NKB = 64

    import contextlib
    with contextlib.ExitStack() as _st:
        ones = _st.enter_context(nc.sbuf_tensor("ones", [128, 128], BF16))
        ident = _st.enter_context(nc.sbuf_tensor("ident", [128, 128], BF16))
        ksb = _st.enter_context(nc.sbuf_tensor("ksb", [128, 2, S], BF16))
        vsb = _st.enter_context(nc.sbuf_tensor("vsb", [128, 2, NKB, 128], BF16))
        qsb = _st.enter_context(nc.sbuf_tensor("qsb", [128, 2, TLOC], BF16))
        esb = _st.enter_context(nc.sbuf_tensor("esb", [128, 2, 1024], BF16))
        inv = _st.enter_context(nc.sbuf_tensor("inv", [128, 1024], F32))
        ao = _st.enter_context(nc.sbuf_tensor("ao", [128, 1024], BF16))
        pS0 = _st.enter_context(nc.psum_tensor("pS0", [128, 1024], F32))
        pS1 = _st.enter_context(nc.psum_tensor("pS1", [128, 1024], F32))
        pO = _st.enter_context(nc.psum_tensor("pO", [128, 1024], F32))
        pU = _st.enter_context(nc.psum_tensor("pU", [128, 1024], F32))
        blk = _st.enter_context(nc.Block())

        p = Prog(nc)
        ld = p.sem("s_ld"); mm = p.sem("s_mm"); sc = p.sem("s_sc")
        vx = p.sem("s_vx"); st = p.sem("s_st"); gc = p.sem("s_gc")
        gld = p.sem("s_gld")

        bS = [Bank(pS0), Bank(pS1)]
        bO = Bank(pO); bU = Bank(pU)

        consts_done = emit_consts(p, gc, ones, ident)
        p.wait("tensor", gc, consts_done)

        head_done = [None] * (NH + 2)
        load_done = [None] * NH

        def emit_head_load(h):
            par = h % 2
            if h >= 2 and head_done[h - 2] is not None:
                p.wait("sync", mm, head_done[h - 2])
                p.wait("gpsimd", mm, head_done[h - 2])
            lv = None
            p.op("sync", lambda sy, h=h, par=par: sy.dma_start(
                out=qsb[:, par, :], in_=qT[h * 128:(h + 1) * 128, :]), ld, 16)
            for r in range(4):
                p.op("sync", lambda sy, h=h, r=r, par=par: sy.dma_start(
                    out=ksb[:, par, r * TLOC:(r + 1) * TLOC],
                    in_=kfull[r, h * 128:(h + 1) * 128, :]), ld, 16)
            for r in range(4):
                for tt in range(NT):
                    kb = r * NT + tt
                    eng = "sync" if kb % 2 == 0 else "gpsimd"
                    sem = ld if kb % 2 == 0 else gld
                    v = p.op(eng, lambda e, r=r, tt=tt, h=h, par=par, kb=kb: e.dma_start(
                        out=vsb[:, par, kb, :], in_=vfull[r, tt, h, :, :]), sem, 16)
                    if kb % 2 == 0:
                        lv = v
                    else:
                        lv2 = v
            load_done[h] = (ld.n, gld.n)

        emit_head_load(0)
        emit_head_load(1)

        esb_free = [None, None]
        for h in range(NH):
            par = h % 2
            p.wait("tensor", ld, load_done[h][0])
            p.wait("tensor", gld, load_done[h][1])
            for qg in range(2):
                q0 = qg * 1024
                bO.wait_free(p)
                bU.wait_free(p)
                for kb in range(NKB):
                    sb = bS[kb % 2]
                    sb.wait_free(p)
                    for half in range(2):
                        mv = p.op("tensor", lambda t, h=h, par=par, kb=kb, q0=q0, half=half, sb=sb: t.matmul(
                            sb.tile[:, half * 512:(half + 1) * 512],
                            ksb[:, par, kb * 128:(kb + 1) * 128],
                            qsb[:, par, q0 + half * 512:q0 + (half + 1) * 512],
                            start=True, stop=True),
                            mm if half == 1 else None)
                    p.wait("scalar", mm, mv)
                    if esb_free[kb % 2] is not None:
                        p.wait("scalar", mm, esb_free[kb % 2])
                    ev = p.op("scalar", lambda s, kb=kb, sb=sb: s.activation(
                        esb[:, kb % 2, :], sb.tile[:, :], AF.Exp, scale=RSQ), sc)
                    sb.set_free(sc, ev)
                    p.wait("tensor", sc, ev)
                    for half in range(2):
                        p.op("tensor", lambda t, kb=kb, par=par, half=half: t.matmul(
                            pO[:, half * 512:(half + 1) * 512],
                            vsb[:, par, kb, :],
                            esb[:, kb % 2, half * 512:(half + 1) * 512],
                            start=(kb == 0), stop=(kb == NKB - 1)))
                        av = p.op("tensor", lambda t, kb=kb, half=half: t.matmul(
                            pU[:, half * 512:(half + 1) * 512],
                            ones[:, :],
                            esb[:, kb % 2, half * 512:(half + 1) * 512],
                            start=(kb == 0), stop=(kb == NKB - 1)),
                            mm if half == 1 else None)
                    esb_free[kb % 2] = av
                p.wait("vector", mm, av)
                rv = p.edged("vector", lambda v: v.reciprocal(inv[:, :], pU[:, :]), vx)
                ov = p.edged("vector", lambda v: v.tensor_tensor(
                    ao[:, :], pO[:, :], inv[:, :], ALU.mult), vx)
                bO.set_free(vx, ov)
                bU.set_free(vx, rv)
                p.wait("sync", vx, ov)
                sv = p.op("sync", lambda sy, h=h, q0=q0: sy.dma_start(
                    out=aoT[h * 128:(h + 1) * 128, q0:q0 + 1024], in_=ao[:, :]), st, 16)
                p.wait("vector", st, sv)
            head_done[h] = mm.n
            if h + 2 < NH:
                emit_head_load(h + 2)
        p.wait("sync", st, st.n)
        p.flush(blk)
    return nc


def build_attn1():
    nc = bass.Bass("TRN2", target_bir_lowering=False, debug=False)
    qT = nc.dram_tensor("qT", [D, TLOC], BF16, kind="ExternalInput").ap()
    kfull = nc.dram_tensor("kfull", [4, D, TLOC], BF16, kind="ExternalInput").ap()
    vfull = nc.dram_tensor("vfull", [4, NT, NFC, 128, 128], BF16, kind="ExternalInput").ap()
    masks = nc.dram_tensor("masks", [4, 128, 128], BF16, kind="ExternalInput").ap()
    aoT = nc.dram_tensor("aoT", [D, TLOC], BF16, kind="ExternalOutput").ap()
    HG = 4

    import contextlib
    with contextlib.ExitStack() as _st:
        ones = _st.enter_context(nc.sbuf_tensor("ones", [128, 128], BF16))
        ident = _st.enter_context(nc.sbuf_tensor("ident", [128, 128], BF16))
        msk = _st.enter_context(nc.sbuf_tensor("msk", [128, 4, 128], BF16))
        ksb = _st.enter_context(nc.sbuf_tensor("ksb", [128, HG, S], BF16))
        vsb = _st.enter_context(nc.sbuf_tensor("vsb", [128, HG, 64, 128], BF16))
        qsb = _st.enter_context(nc.sbuf_tensor("qsb", [128, HG, TLOC], BF16))
        esb = _st.enter_context(nc.sbuf_tensor("esb", [128, 2, 512], BF16))
        inv = _st.enter_context(nc.sbuf_tensor("inv", [128, 512], F32))
        ao = _st.enter_context(nc.sbuf_tensor("ao", [128, 512], BF16))
        pS0 = _st.enter_context(nc.psum_tensor("pS0", [128, 512], F32))
        pS1 = _st.enter_context(nc.psum_tensor("pS1", [128, 512], F32))
        pO0 = _st.enter_context(nc.psum_tensor("pO0", [128, 512], F32))
        pO1 = _st.enter_context(nc.psum_tensor("pO1", [128, 512], F32))
        pO2 = _st.enter_context(nc.psum_tensor("pO2", [128, 512], F32))
        pO3 = _st.enter_context(nc.psum_tensor("pO3", [128, 512], F32))
        pU0 = _st.enter_context(nc.psum_tensor("pU0", [128, 512], F32))
        pU1 = _st.enter_context(nc.psum_tensor("pU1", [128, 512], F32))
        blk = _st.enter_context(nc.Block())

        p = Prog(nc)
        ld = p.sem("s_ld"); mm = p.sem("s_mm"); sc = p.sem("s_sc")
        vx = p.sem("s_vx"); st = p.sem("s_st"); gc = p.sem("s_gc")
        gld = p.sem("s_gld")

        bS = [Bank(pS0), Bank(pS1)]
        bO = [Bank(pO0), Bank(pO1), Bank(pO2), Bank(pO3)]
        bU = [Bank(pU0), Bank(pU1)]

        consts_done = emit_consts(p, gc, ones, ident)
        p.wait("tensor", gc, consts_done)
        for r in range(4):
            mload = p.op("sync", lambda sy, r=r: sy.dma_start(
                out=msk[:, r, :], in_=masks[r, :, :]), ld, 16)

        hg_done = [None] * (HG + 2)

        def emit_group_load(hg):
            if hg >= 1 and hg_done[hg - 1] is not None:
                p.wait("sync", mm, hg_done[hg - 1])
                p.wait("gpsimd", mm, hg_done[hg - 1])
            for hi in range(HG):
                h = hg * HG + hi
                p.op("sync", lambda sy, h=h, hi=hi: sy.dma_start(
                    out=qsb[:, hi, :], in_=qT[h * 128:(h + 1) * 128, :]), ld, 16)
                for r in range(4):
                    p.op("sync", lambda sy, h=h, hi=hi, r=r: sy.dma_start(
                        out=ksb[:, hi, r * TLOC:(r + 1) * TLOC],
                        in_=kfull[r, h * 128:(h + 1) * 128, :]), ld, 16)
                for r in range(4):
                    for tt in range(NT):
                        kb = r * NT + tt
                        eng, sem = (("sync", ld) if kb % 2 == 0 else ("gpsimd", gld))
                        p.op(eng, lambda e, r=r, tt=tt, h=h, hi=hi, kb=kb: e.dma_start(
                            out=vsb[:, hi, kb, :], in_=vfull[r, tt, h, :, :]), sem, 16)
            return (ld.n, gld.n)

        esb_free = [None, None]
        for hg in range(NH // HG):
            lv = emit_group_load(hg)
            p.wait("tensor", ld, lv[0])
            p.wait("tensor", gld, lv[1])
            if hg == 0:
                p.wait("scalar", ld, mload)
                p.wait("vector", ld, mload)
            for t in range(NT):
                kbs = [(r * NT + tt, None) for tt in range(t) for r in range(4)]
                kbs += [(r * NT + t, r) for r in range(4)]
                bo = bO[t % 4]
                bu = bU[t % 2]
                bo.wait_free(p)
                bu.wait_free(p)
                for ki, (kb, mr) in enumerate(kbs):
                    first = ki == 0
                    last = ki == len(kbs) - 1
                    sb = bS[ki % 2]
                    sb.wait_free(p)
                    for hi in range(HG):
                        mv = p.op("tensor", lambda tt_, hi=hi, kb=kb, t=t, sb=sb: tt_.matmul(
                            sb.tile[:, hi * 128:(hi + 1) * 128],
                            ksb[:, hi, kb * 128:(kb + 1) * 128],
                            qsb[:, hi, t * 128:(t + 1) * 128],
                            start=True, stop=True),
                            mm if hi == HG - 1 else None)
                    p.wait("scalar", mm, mv)
                    if esb_free[ki % 2] is not None:
                        p.wait("scalar", mm, esb_free[ki % 2])
                    ev = p.op("scalar", lambda s, ki=ki, sb=sb: s.activation(
                        esb[:, ki % 2, :], sb.tile[:, :], AF.Exp, scale=RSQ), sc)
                    sb.set_free(sc, ev)
                    cons = ("tensor", sc, ev)
                    if mr is not None:
                        p.wait("vector", sc, ev)
                        tvs = []
                        for hi in range(HG):
                            tv = p.edged("vector", lambda v, ki=ki, hi=hi, mr=mr: v.tensor_tensor(
                                esb[:, ki % 2, hi * 128:(hi + 1) * 128],
                                esb[:, ki % 2, hi * 128:(hi + 1) * 128],
                                msk[:, mr, :], ALU.mult), vx)
                        cons = ("tensor", vx, tv)
                    p.wait(cons[0], cons[1], cons[2])
                    for hi in range(HG):
                        p.op("tensor", lambda tt_, ki=ki, kb=kb, hi=hi, bo=bo, first=first, last=last: tt_.matmul(
                            bo.tile[:, hi * 128:(hi + 1) * 128],
                            vsb[:, hi, kb, :],
                            esb[:, ki % 2, hi * 128:(hi + 1) * 128],
                            start=(first and hi == 0), stop=(last and hi == HG - 1),
                            skip_group_check=True))
                        av = p.op("tensor", lambda tt_, ki=ki, hi=hi, bu=bu, first=first, last=last: tt_.matmul(
                            bu.tile[:, hi * 128:(hi + 1) * 128],
                            ones[:, :],
                            esb[:, ki % 2, hi * 128:(hi + 1) * 128],
                            start=(first and hi == 0), stop=(last and hi == HG - 1),
                            skip_group_check=True),
                            mm if hi == HG - 1 else None)
                    esb_free[ki % 2] = av
                p.wait("vector", mm, av)
                rv = p.edged("vector", lambda v, bu=bu: v.reciprocal(inv[:, :], bu.tile[:, :]), vx)
                ov = p.edged("vector", lambda v, bo=bo: v.tensor_tensor(
                    ao[:, :], bo.tile[:, :], inv[:, :], ALU.mult), vx)
                bo.set_free(vx, ov)
                bu.set_free(vx, rv)
                p.wait("sync", vx, ov)
                sv = None
                for hi in range(HG):
                    h = hg * HG + hi
                    sv = p.op("sync", lambda sy, h=h, hi=hi, t=t: sy.dma_start(
                        out=aoT[h * 128:(h + 1) * 128, t * 128:(t + 1) * 128],
                        in_=ao[:, hi * 128:(hi + 1) * 128]), st, 16)
                p.wait("vector", st, sv)
            hg_done[hg] = mm.n
        p.wait("sync", st, st.n)
        p.flush(blk)
    return nc



def build_proj(expand, dbg=False):
    nc = bass.Bass("TRN2", target_bir_lowering=False, debug=False)
    aoT = nc.dram_tensor("aoT", [D, TLOC], BF16, kind="ExternalInput").ap()
    xres_i = nc.dram_tensor("xres", [D, TLOC], BF16, kind="ExternalInput").ap()
    if expand:
        w_c = nc.dram_tensor("w_codes", [D, D], F32, kind="ExternalInput").ap()
        w_s = nc.dram_tensor("w_scales", [64, D], F32, kind="ExternalInput").ap()
    else:
        w_f = nc.dram_tensor("w", [D, D], F32, kind="ExternalInput").ap()
    b_in = nc.dram_tensor("b", [128, NFC], F32, kind="ExternalInput").ap()
    xout = nc.dram_tensor("xout", [D, TLOC], BF16, kind="ExternalOutput").ap()
    if dbg:
        ydram = nc.dram_tensor("ydram", [D, TLOC], BF16, kind="ExternalOutput")
        mbo = nc.dram_tensor("mbo", [128, TLOC], F32, kind="ExternalOutput").ap()
        rbo = nc.dram_tensor("rbo", [128, TLOC], F32, kind="ExternalOutput").ap()
    else:
        ydram = nc.dram_tensor("ydram", [D, TLOC], BF16)

    import contextlib
    with contextlib.ExitStack() as _st:
        ones = _st.enter_context(nc.sbuf_tensor("ones", [128, 128], BF16))
        ident = _st.enter_context(nc.sbuf_tensor("ident", [128, 128], BF16))
        eps_t = _st.enter_context(nc.sbuf_tensor("eps_t", [128, 1], F32))
        bsel = _st.enter_context(nc.sbuf_tensor("bsel", [128, 64, 32], BF16))
        asb = _st.enter_context(nc.sbuf_tensor("asb", [128, NFC, TLOC], BF16))
        wexp = _st.enter_context(nc.sbuf_tensor("wexp", [128, NFC, D], BF16))
        mb = _st.enter_context(nc.sbuf_tensor("mb", [128, TLOC], F32))
        rb = _st.enter_context(nc.sbuf_tensor("rb", [128, TLOC], F32))
        sq = _st.enter_context(nc.sbuf_tensor("sq", [128, 2, 512], BF16))
        codes = _st.enter_context(nc.sbuf_tensor("codes", [128, 2, D], F32))
        scal = _st.enter_context(nc.sbuf_tensor("scal", [128, D], BF16))
        b_sb = _st.enter_context(nc.sbuf_tensor("b_sb", [128, NFC], F32))
        stage = _st.enter_context(nc.sbuf_tensor("stage", [128, 2, 512], BF16))
        tmp = _st.enter_context(nc.sbuf_tensor("tmp", [128, 2, 512], BF16))
        xres_sb = _st.enter_context(nc.sbuf_tensor("xres_sb", [128, 2, 512], BF16))
        pA = _st.enter_context(nc.psum_tensor("pA", [128, 512], F32))
        pB = _st.enter_context(nc.psum_tensor("pB", [128, 512], F32))
        pe0 = _st.enter_context(nc.psum_tensor("pe0", [128, 512], F32))
        pe1 = _st.enter_context(nc.psum_tensor("pe1", [128, 512], F32))
        pp0 = _st.enter_context(nc.psum_tensor("pp0", [128, 512], F32))
        pp1 = _st.enter_context(nc.psum_tensor("pp1", [128, 512], F32))
        blk = _st.enter_context(nc.Block())

        p = Prog(nc)
        ld = p.sem("s_ld"); gld = p.sem("s_gld"); mm = p.sem("s_mm")
        sc = p.sem("s_sc"); vx = p.sem("s_vx"); st = p.sem("s_st")
        gc = p.sem("s_gc")
        sems = dict(mm=mm, sc=sc, vx=vx)
        bA = Bank(pA); bB = Bank(pB)
        bE = [Bank(pe0), Bank(pe1)]
        bP = [Bank(pp0), Bank(pp1)]

        consts_done = emit_consts(p, gc, ones, ident, eps_t=eps_t)
        if expand:
            p.op("gpsimd", lambda g: g.memset(scal[:, :], 0.0))
            p.op("gpsimd", lambda g: g.memset(bsel[:, :, :], 1.0))
            bsel_done = p.op("gpsimd", lambda g: g.affine_select(
                out=bsel[:, :, :], in_=bsel[:, :, :], compare_op=ALU.is_equal,
                fill=0.0, base=0, pattern=[[1, 64], [0, 32]],
                channel_multiplier=-1), gc)

        for fc in range(NFC):
            p.op("sync", lambda sy, fc=fc: sy.dma_start(
                out=asb[:, fc, :], in_=aoT[fc * 128:(fc + 1) * 128, :]), ld, 16)
        ao_ld = ld.n
        p.op("sync", lambda sy: sy.dma_start(out=b_sb[:, :], in_=b_in), ld, 16)
        b_ld = ld.n

        p.wait("tensor", gc, consts_done)
        if expand:
            p.wait("vector", mm, mm.n)
            sv = p.op("gpsimd", lambda g: g.dma_start(out=scal[0:64, :], in_=w_s), gld, 16)
            cl = {}
            cfree = [None, None]
            def load_codes(hc):
                par = hc % 2
                if cfree[par] is not None:
                    p.wait("sync", vx, cfree[par])
                cl[hc] = p.op("sync", lambda sy, hc=hc, par=par: sy.dma_start(
                    out=codes[:, par, :], in_=w_c[hc * 128:(hc + 1) * 128, :]), ld, 16)
            load_codes(0); load_codes(1)
            wv = None
            for hc in range(NFC):
                par = hc % 2
                p.wait("tensor", ld, cl[hc])
                p.wait("tensor", gld, sv)
                p.wait("vector", ld, cl[hc])
                p.wait("tensor", gc, bsel_done)
                for c4 in range(4):
                    bank = bE[c4 % 2]
                    bank.wait_free(p)
                    mv = p.op("tensor", lambda t, hc=hc, c4=c4, bank=bank: t.matmul(
                        bank.tile[:, :], bsel[:, hc * 4:(hc + 1) * 4, :].rearrange("p a b -> p (a b)"),
                        scal[:, c4 * 512:(c4 + 1) * 512], start=True, stop=True), mm)
                    p.wait("vector", mm, mv)
                    wv = p.edged("vector", lambda v, hc=hc, c4=c4, par=par, bank=bank: v.tensor_tensor(
                        wexp[:, hc, c4 * 512:(c4 + 1) * 512],
                        codes[:, par, c4 * 512:(c4 + 1) * 512],
                        bank.tile[:, :], ALU.mult), vx)
                    bank.set_free(vx, wv)
                cfree[par] = wv
                if hc + 2 < NFC:
                    load_codes(hc + 2)
            w_ready = ("vector", vx, wv)
        else:
            wl = None
            for hc in range(NFC):
                wl = p.op("gpsimd", lambda g, hc=hc: g.dma_start(
                    out=wexp[:, hc, :], in_=w_f[hc * 128:(hc + 1) * 128, :]), gld, 16)
            w_ready = ("gpsimd-dma", gld, wl)

        p.wait("tensor", w_ready[1], w_ready[2])
        p.wait("tensor", ld, ao_ld)
        stage_free = [None, None]
        sq_free = [None, None]
        stats_v = []
        for tc in range(NTC):
            t0 = tc * 512
            bA.wait_free(p)
            bB.wait_free(p)
            mv2 = None
            for fc in range(NFC):
                bank = bP[fc % 2]
                bank.wait_free(p)
                for hc in range(NFC):
                    mv = p.op("tensor", lambda t, hc=hc, fc=fc, t0=t0, bank=bank: t.matmul(
                        bank.tile[:, :], wexp[:, hc, fc * 128:(fc + 1) * 128],
                        asb[:, hc, t0:t0 + 512],
                        start=(hc == 0), stop=(hc == NFC - 1)),
                        mm if hc == NFC - 1 else None)
                spar = fc % 2
                p.wait("scalar", mm, mv)
                if fc == 0:
                    p.wait("scalar", ld, b_ld)
                if stage_free[spar] is not None:
                    p.wait("scalar", st, stage_free[spar])
                if sq_free[spar] is not None:
                    p.wait("scalar", mm, sq_free[spar])
                ev = p.op("scalar", lambda s, fc=fc, bank=bank, spar=spar: s.activation(
                    stage[:, spar, :], bank.tile[:, :], AF.Identity,
                    bias=b_sb[:, fc:fc + 1]), sc)
                bank.set_free(sc, ev)
                sv2 = p.op("scalar", lambda s, spar=spar: s.activation(
                    sq[:, spar, :], stage[:, spar, :], AF.Square), sc)
                p.wait("tensor", sc, sv2)
                p.op("tensor", lambda t, spar=spar, fc=fc: t.matmul(
                    bA.tile[:, :], ones[:, :], stage[:, spar, :],
                    start=(fc == 0), stop=(fc == NFC - 1)))
                mv2 = p.op("tensor", lambda t, spar=spar, fc=fc: t.matmul(
                    bB.tile[:, :], ones[:, :], sq[:, spar, :],
                    start=(fc == 0), stop=(fc == NFC - 1)), mm)
                sq_free[spar] = mv2
                p.wait("sync", sc, ev)
                dv = p.op("sync", lambda sy, fc=fc, t0=t0, spar=spar: sy.dma_start(
                    out=ydram.ap()[fc * 128:(fc + 1) * 128, t0:t0 + 512],
                    in_=stage[:, spar, :]), st, 16)
                stage_free[spar] = dv
            p.wait("vector", mm, mv2)
            v1 = p.edged("vector", lambda v, t0=t0: v.tensor_scalar_mul(
                mb[:, t0:t0 + 512], bA.tile[:, :], 1.0 / D), vx)
            p.edged("vector", lambda v, t0=t0: v.scalar_tensor_tensor(
                rb[:, t0:t0 + 512], mb[:, t0:t0 + 512], -1.0, mb[:, t0:t0 + 512],
                ALU.mult, ALU.mult), vx)
            v3 = p.edged("vector", lambda v, t0=t0: v.scalar_tensor_tensor(
                rb[:, t0:t0 + 512], bB.tile[:, :], 1.0 / D, rb[:, t0:t0 + 512],
                ALU.mult, ALU.add), vx)
            bA.set_free(vx, v1)
            bB.set_free(vx, v3)
            p.wait("scalar", vx, v3)
            p.edged("scalar", lambda s, t0=t0: s.activation(
                rb[:, t0:t0 + 512], rb[:, t0:t0 + 512], AF.Ln, bias=eps_t[:, 0:1]), sc)
            p.edged("scalar", lambda s, t0=t0: s.activation(
                rb[:, t0:t0 + 512], rb[:, t0:t0 + 512], AF.Exp, scale=-0.5), sc)
        stats_done = sc.n

        p.wait("vector", sc, stats_done)
        p.wait("sync", st, st.n)
        p.wait("sync", sc, sc.n)
        if dbg:
            p.op("sync", lambda sy: sy.dma_start(out=mbo, in_=mb[:, :]), st, 16)
            p.op("sync", lambda sy: sy.dma_start(out=rbo, in_=rb[:, :]), st, 16)
        yld_free = [None, None]
        xres_free2 = [None, None]
        for fc in range(NFC):
            for tc in range(NTC):
                t0 = tc * 512
                par = (fc * NTC + tc) % 2
                if yld_free[par] is not None:
                    p.wait("sync", vx, yld_free[par])
                lv = p.op("sync", lambda sy, fc=fc, t0=t0, par=par: sy.dma_start(
                    out=stage[:, par, :], in_=ydram.ap()[fc * 128:fc * 128 + 128, t0:t0 + 512]), ld, 16)
                if xres_free2[par] is not None:
                    p.wait("sync", vx, xres_free2[par])
                lx = p.op("sync", lambda sy, fc=fc, t0=t0, par=par: sy.dma_start(
                    out=xres_sb[:, par, :], in_=xres_i[fc * 128:fc * 128 + 128, t0:t0 + 512]), ld, 16)
                p.wait("vector", ld, lx)
                p.edged("vector", lambda v, fc=fc, t0=t0, par=par: v.tensor_tensor(
                    tmp[:, 0, :], stage[:, par, :], mb[:, t0:t0 + 512], ALU.subtract), vx)
                yv = p.edged("vector", lambda v, t0=t0: v.tensor_tensor(
                    tmp[:, 0, :], tmp[:, 0, :], rb[:, t0:t0 + 512], ALU.mult), vx)
                yld_free[par] = yv
                ov = p.edged("vector", lambda v, par=par: v.tensor_tensor(
                    tmp[:, 1, :], tmp[:, 0, :], xres_sb[:, par, :], ALU.add), vx)
                xres_free2[par] = ov
                p.wait("sync", vx, ov)
                sv3 = p.op("sync", lambda sy, fc=fc, t0=t0: sy.dma_start(
                    out=xout[fc * 128:fc * 128 + 128, t0:t0 + 512], in_=tmp[:, 1, :]), st, 16)
                p.wait("vector", st, sv3)
        p.wait("sync", st, st.n)
        p.flush(blk)
    return nc



IM = 1024
NICB = IM // 128
NEXP = 16
SCALE_MOE = 2.5


def build_moe(dbg=False):
    nc = bass.Bass("TRN2", target_bir_lowering=False, debug=False)
    x3 = nc.dram_tensor("x3", [D, TLOC], BF16, kind="ExternalInput").ap()
    gate_w = nc.dram_tensor("gate_w", [NEXP, D], F32, kind="ExternalInput").ap()
    Wg = nc.dram_tensor("Wg", [NEXP, D, IM], F32, kind="ExternalInput").ap()
    Wu = nc.dram_tensor("Wu", [NEXP, D, IM], F32, kind="ExternalInput").ap()
    Wd = nc.dram_tensor("Wd", [NEXP, IM, D], F32, kind="ExternalInput").ap()
    sWg = nc.dram_tensor("sWg", [D, D], F32, kind="ExternalInput").ap()
    sWu = nc.dram_tensor("sWu", [D, D], F32, kind="ExternalInput").ap()
    sWd = nc.dram_tensor("sWd", [D, D], F32, kind="ExternalInput").ap()
    out_o = nc.dram_tensor("out", [D, TLOC], F32, kind="ExternalOutput").ap()
    if dbg:
        combo_o = nc.dram_tensor("combo", [128, NT * 16], F32, kind="ExternalOutput").ap()
        rwd_o = nc.dram_tensor("rwdo", [128, NT * 16], F32, kind="ExternalOutput").ap()
        xlnd_o = nc.dram_tensor("xlnd", [D, TLOC], BF16, kind="ExternalOutput").ap()
        gwtd_o = nc.dram_tensor("gwtd", [128, NFC * 16], BF16, kind="ExternalOutput").ap()
        accd_o = nc.dram_tensor("accd", [D, TLOC], BF16, kind="ExternalOutput").ap()

    import contextlib
    with contextlib.ExitStack() as _st:
        ones = _st.enter_context(nc.sbuf_tensor("ones", [128, 128], BF16))
        ident = _st.enter_context(nc.sbuf_tensor("ident", [128, 128], BF16))
        identf = _st.enter_context(nc.sbuf_tensor("identf", [128, 128], F32))
        onesf = _st.enter_context(nc.sbuf_tensor("onesf", [1, 128], F32))
        eps_t = _st.enter_context(nc.sbuf_tensor("eps_t", [128, 1], F32))
        xln = _st.enter_context(nc.sbuf_tensor("xln", [128, NFC, TLOC], BF16))
        acc = _st.enter_context(nc.sbuf_tensor("acc", [128, NFC, TLOC], BF16))
        ssb = _st.enter_context(nc.sbuf_tensor("ssb", [128, NICB, TLOC], BF16))
        wgsb = _st.enter_context(nc.sbuf_tensor("wgsb", [128, 2, NFC, 128], BF16))
        wdsb = _st.enter_context(nc.sbuf_tensor("wdsb", [128, 2, NICB, 128], BF16))
        usb = _st.enter_context(nc.sbuf_tensor("usb", [128, 2, 512], BF16))
        gsb = _st.enter_context(nc.sbuf_tensor("gsb", [128, D], BF16))
        gwT = _st.enter_context(nc.sbuf_tensor("gwT", [128, NFC, 16], BF16))
        rw = _st.enter_context(nc.sbuf_tensor("rw", [128, 16], F32))
        work = _st.enter_context(nc.sbuf_tensor("work", [128, 16], F32))
        selt = _st.enter_context(nc.sbuf_tensor("selt", [128, 16], F32))
        mth = _st.enter_context(nc.sbuf_tensor("mth", [128, 1], F32))
        combtok = _st.enter_context(nc.sbuf_tensor("combtok", [128, NT, 16], F32))
        rwd = _st.enter_context(nc.sbuf_tensor("rwd", [128, NT, 16], F32))
        combR = _st.enter_context(nc.sbuf_tensor("combR", [1, 512], F32))
        mb = _st.enter_context(nc.sbuf_tensor("mb", [128, TLOC], BF16))
        rb = _st.enter_context(nc.sbuf_tensor("rb", [128, TLOC], BF16))
        sq = _st.enter_context(nc.sbuf_tensor("sq", [128, 2, 512], BF16))
        tmp = _st.enter_context(nc.sbuf_tensor("tmp", [128, 2, 512], BF16))
        xres_sb = _st.enter_context(nc.sbuf_tensor("xres_sb", [128, 2, 512], BF16))
        outst = _st.enter_context(nc.sbuf_tensor("outst", [128, 2, 512], F32))
        pA = _st.enter_context(nc.psum_tensor("pA", [128, 512], F32))
        pB = _st.enter_context(nc.psum_tensor("pB", [128, 512], F32))
        pe0 = _st.enter_context(nc.psum_tensor("pe0", [128, 512], F32))
        pe1 = _st.enter_context(nc.psum_tensor("pe1", [128, 512], F32))
        pp0 = _st.enter_context(nc.psum_tensor("pp0", [128, 512], F32))
        pp1 = _st.enter_context(nc.psum_tensor("pp1", [128, 512], F32))
        ptr = _st.enter_context(nc.psum_tensor("ptr", [128, 1024], BF16))
        ptf = _st.enter_context(nc.psum_tensor("ptf", [128, 512], F32))
        blk = _st.enter_context(nc.Block())

        p = Prog(nc)
        ld = p.sem("s_ld"); gld = p.sem("s_gld"); mm = p.sem("s_mm")
        sc = p.sem("s_sc"); vx = p.sem("s_vx"); st = p.sem("s_st")
        gc = p.sem("s_gc")
        sems = dict(mm=mm, sc=sc, vx=vx)
        bA = Bank(pA); bB = Bank(pB)
        bE = [Bank(pe0), Bank(pe1)]
        bP = [Bank(pp0), Bank(pp1)]
        bT = Bank(ptr); bF = Bank(ptf)

        consts_done = emit_consts(p, gc, ones, ident, eps_t=eps_t)
        p.op("gpsimd", lambda g: g.memset(onesf[:, :], 1.0))
        p.op("gpsimd", lambda g: g.memset(identf[:, :], 0.0))
        consts_done = p.op("gpsimd", lambda g: g.affine_select(
            out=identf[:, :], in_=identf[:, :], compare_op=ALU.not_equal,
            fill=1.0, base=0, pattern=[[-1, 128]], channel_multiplier=1), gc)

        for fc in range(NFC):
            p.op("sync", lambda sy, fc=fc: sy.dma_start(
                out=xln[:, fc, :], in_=x3[fc * 128:(fc + 1) * 128, :]), ld, 16)
        p.wait("scalar", ld, 16 * NFC)
        p.wait("tensor", ld, 16 * NFC)
        stats_ready = emit_ln_stats(
            p, sems, lambda fc: xln[:, fc, :], TLOC, ones, mb, rb, bA, bB,
            sq, eps_t, extra_tensor_wait=(gc, consts_done))
        xln_ready = emit_normalize_inplace(
            p, sems, xln, mb, rb, tmp[:, 0, :], stats_ready,
            [(mm, mm.n), (sc, sc.n)])

        gz = p.op("gpsimd", lambda g: g.memset(gsb[:, :], 0.0), gc)
        p.wait("gpsimd", gc, gz)
        gwl = p.op("gpsimd", lambda g: g.dma_start(out=gsb[0:16, :], in_=gate_w), gld, 16)
        p.wait("tensor", gld, gwl)
        for hc in range(NFC):
            bT.wait_free(p)
            tv = p.op("tensor", lambda t, hc=hc: t.transpose(
                ptr[:, 0:128], gsb[:, hc * 128:(hc + 1) * 128], ident[:, :]), mm)
            p.wait("vector", mm, tv)
            cv = p.edged("vector", lambda v, hc=hc: v.tensor_copy(
                gwT[:, hc, :], ptr[:, 0:16]), vx)
            bT.set_free(vx, cv)
        gwT_v = vx.n
        p.wait("tensor", vx, gwT_v)
        p.wait("tensor", vx, xln_ready)
        rw_free = None
        for tt in range(NT):
            bA.wait_free(p)
            for hc in range(NFC):
                lv = p.op("tensor", lambda t, tt=tt, hc=hc: t.matmul(
                    pA[:, 0:16], xln[:, hc, tt * 128:(tt + 1) * 128],
                    gwT[:, hc, :], start=(hc == 0), stop=(hc == NFC - 1)),
                    mm if hc == NFC - 1 else None)
            p.wait("scalar", mm, lv)
            if rw_free is not None:
                p.wait("scalar", vx, rw_free)
            sgv = p.edged("scalar", lambda s: s.activation(
                rw[:, :], pA[:, 0:16], AF.Sigmoid), sc)
            bA.set_free(sc, sgv)
            p.wait("vector", sc, sgv)
            p.edged("vector", lambda v, tt=tt: v.tensor_copy(rwd[:, tt, :], rw[:, :]), vx)
            p.edged("vector", lambda v: v.tensor_copy(work[:, :], rw[:, :]), vx)
            for it in range(8):
                p.edged("vector", lambda v: v.reduce_max(
                    mth[:, :], work[:, :], axis=AX), vx)
                if it < 7:
                    p.edged("vector", lambda v: v.tensor_scalar(
                        selt[:, :], work[:, :], mth[:, 0:1], -1e30,
                        ALU.is_ge, ALU.mult), vx)
                    p.edged("vector", lambda v: v.tensor_add(
                        work[:, :], work[:, :], selt[:, :]), vx)
            p.edged("vector", lambda v: v.tensor_scalar(
                selt[:, :], rw[:, :], mth[:, 0:1], None, ALU.is_ge), vx)
            p.edged("vector", lambda v: v.tensor_mul(work[:, :], rw[:, :], selt[:, :]), vx)
            p.edged("vector", lambda v: v.reduce_sum(mth[:, :], work[:, :], axis=AX), vx)
            p.edged("vector", lambda v: v.tensor_scalar_add(mth[:, :], mth[:, :], 1e-6), vx)
            p.edged("vector", lambda v: v.reciprocal(mth[:, :], mth[:, :]), vx)
            cbv = p.edged("vector", lambda v: v.tensor_scalar(
                selt[:, :], work[:, :], mth[:, 0:1], SCALE_MOE, ALU.mult, ALU.mult), vx)
            cpv = p.edged("vector", lambda v, tt=tt: v.tensor_copy(
                combtok[:, tt, :], selt[:, :]), vx)
            rw_free = cpv
        combT_v = vx.n

        jobs = [("sh", 0), ("sh", 1)] + [("e", e) for e in range(NEXP)]
        wg_free = [None, None]
        wd_free = [None, None]
        usb_free = [None, None]
        tmp_free = [None, None]
        acc_free = {}
        prev_down_mm = None
        for ji, (kind, idx) in enumerate(jobs):
            if kind == "sh":
                wg_ap = lambda icb, idx=idx: sWg[:, idx * IM + icb * 128: idx * IM + (icb + 1) * 128]
                wu_ap = lambda icb, idx=idx: sWu[:, idx * IM + icb * 128: idx * IM + (icb + 1) * 128]
                wd_ap = lambda hc, idx=idx: sWd[idx * IM:(idx + 1) * IM, hc * 128:(hc + 1) * 128]
                has_comb = False
            else:
                wg_ap = lambda icb, idx=idx: Wg[idx, :, icb * 128:(icb + 1) * 128]
                wu_ap = lambda icb, idx=idx: Wu[idx, :, icb * 128:(icb + 1) * 128]
                wd_ap = lambda hc, idx=idx: Wd[idx, :, hc * 128:(hc + 1) * 128]
                has_comb = True

            for which, w_ap, act in (("g", wg_ap, AF.Silu), ("u", wu_ap, AF.Identity)):
                for icb in range(NICB):
                    par = icb % 2
                    if wg_free[par] is not None:
                        p.wait("gpsimd", mm, wg_free[par])
                    wl = p.op("gpsimd", lambda g, w_ap=w_ap, icb=icb, par=par: g.dma_start(
                        out=wgsb[:, par, :, :],
                        in_=w_ap(icb).rearrange("(a p) b -> p a b", p=128)), gld, 16)
                    p.wait("tensor", gld, wl)
                    if ji == 0 and which == "g" and icb == 0:
                        p.wait("tensor", vx, xln_ready)
                    wmv = None
                    for tc in range(NTC):
                        t0 = tc * 512
                        bank = bE[tc % 2]
                        bank.wait_free(p)
                        for hc in range(NFC):
                            wmv = p.op("tensor", lambda t, hc=hc, par=par, t0=t0, bank=bank: t.matmul(
                                bank.tile[:, :], wgsb[:, par, hc, :],
                                xln[:, hc, t0:t0 + 512],
                                start=(hc == 0), stop=(hc == NFC - 1)),
                                mm if hc == NFC - 1 else None)
                        p.wait("scalar", mm, wmv)
                        if which == "g":
                            if prev_down_mm is not None and tc == 0 and icb == 0:
                                p.wait("scalar", mm, prev_down_mm)
                            ev = p.op("scalar", lambda s, icb=icb, t0=t0, bank=bank: s.activation(
                                ssb[:, icb, t0:t0 + 512], bank.tile[:, :], AF.Silu), sc)
                            bank.set_free(sc, ev)
                        else:
                            upar = (icb * NTC + tc) % 2
                            if usb_free[upar] is not None:
                                p.wait("scalar", vx, usb_free[upar])
                            ev = p.op("scalar", lambda s, upar=upar, bank=bank: s.activation(
                                usb[:, upar, :], bank.tile[:, :], AF.Identity), sc)
                            bank.set_free(sc, ev)
                            p.wait("vector", sc, ev)
                            uv = p.edged("vector", lambda v, icb=icb, t0=t0, upar=upar: v.tensor_tensor(
                                ssb[:, icb, t0:t0 + 512], ssb[:, icb, t0:t0 + 512],
                                usb[:, upar, :], ALU.mult), vx)
                            usb_free[upar] = uv
                    wg_free[par] = wmv
            if has_comb:
                p.wait("tensor", vx, combT_v)
                for tc in range(NTC):
                    t0 = tc * 512
                    bF.wait_free(p)
                    ttv = None
                    for k4 in range(4):
                        tt = tc * 4 + k4
                        ttv = p.op("tensor", lambda t, tt=tt, k4=k4, idx=idx: t.transpose(
                            ptf[0:1, k4 * 128:(k4 + 1) * 128],
                            combtok[:, tt, idx:idx + 1], identf[:, :]),
                            mm if k4 == 3 else None)
                    p.wait("vector", mm, ttv)
                    crv = p.edged("vector", lambda v: v.tensor_copy(
                        combR[0:1, :], ptf[0:1, 0:512]), vx)
                    bF.set_free(vx, crv)
                    bB.wait_free(p)
                    p.wait("tensor", vx, crv)
                    cmv = p.op("tensor", lambda t: t.matmul(
                        pB[:, :], onesf[0:1, 0:128], combR[0:1, :],
                        start=True, stop=True), mm)
                    p.wait("vector", mm, cmv)
                    cv2 = None
                    for icb in range(NICB):
                        cv2 = p.edged("vector", lambda v, icb=icb, t0=t0: v.tensor_tensor(
                            ssb[:, icb, t0:t0 + 512], ssb[:, icb, t0:t0 + 512],
                            pB[:, :], ALU.mult), vx)
                    bB.set_free(vx, cv2)
            s_ready = vx.n
            p.wait("tensor", vx, s_ready)
            dmv = None
            for hc in range(NFC):
                par = hc % 2
                if wd_free[par] is not None:
                    p.wait("gpsimd", mm, wd_free[par])
                dl = p.op("gpsimd", lambda g, wd_ap=wd_ap, hc=hc, par=par: g.dma_start(
                    out=wdsb[:, par, :, :],
                    in_=wd_ap(hc).rearrange("(a p) b -> p a b", p=128)), gld, 16)
                p.wait("tensor", gld, dl)
                for tc in range(NTC):
                    t0 = tc * 512
                    bank = bP[tc % 2]
                    bank.wait_free(p)
                    for icb in range(NICB):
                        dmv = p.op("tensor", lambda t, icb=icb, par=par, t0=t0, bank=bank: t.matmul(
                            bank.tile[:, :], wdsb[:, par, icb, :],
                            ssb[:, icb, t0:t0 + 512],
                            start=(icb == 0), stop=(icb == NICB - 1)),
                            mm if icb == NICB - 1 else None)
                    p.wait("scalar", mm, dmv)
                    if ji == 0:
                        ev = p.op("scalar", lambda s, hc=hc, t0=t0, bank=bank: s.activation(
                            acc[:, hc, t0:t0 + 512], bank.tile[:, :], AF.Identity), sc)
                        bank.set_free(sc, ev)
                        acc_free[(hc, tc)] = None
                    else:
                        tpar = (hc * NTC + tc) % 2
                        if tmp_free[tpar] is not None:
                            p.wait("scalar", vx, tmp_free[tpar])
                        ev = p.op("scalar", lambda s, tpar=tpar, bank=bank: s.activation(
                            tmp[:, tpar, :], bank.tile[:, :], AF.Identity), sc)
                        bank.set_free(sc, ev)
                        p.wait("vector", sc, ev)
                        if acc_free.get((hc, tc)) is not None:
                            p.wait("vector", sc, acc_free[(hc, tc)])
                        avv = p.edged("vector", lambda v, hc=hc, t0=t0, tpar=tpar: v.tensor_tensor(
                            acc[:, hc, t0:t0 + 512], acc[:, hc, t0:t0 + 512],
                            tmp[:, tpar, :], ALU.add), vx)
                        tmp_free[tpar] = avv
                wd_free[par] = dmv
            prev_down_mm = mm.n

        p.barrier()
        if dbg:
            p.op("sync", lambda sy: sy.dma_start(
                out=combo_o, in_=combtok[:, :, :].rearrange("p a b -> p (a b)")), st, 16)
            p.op("sync", lambda sy: sy.dma_start(
                out=rwd_o, in_=rwd[:, :, :].rearrange("p a b -> p (a b)")), st, 16)
            for fc in range(NFC):
                p.op("sync", lambda sy, fc=fc: sy.dma_start(
                    out=xlnd_o[fc * 128:(fc + 1) * 128, :], in_=xln[:, fc, :]), st, 16)
            p.op("sync", lambda sy: sy.dma_start(
                out=gwtd_o, in_=gwT[:, :, :].rearrange("p a b -> p (a b)")), st, 16)
            for fc in range(NFC):
                p.op("sync", lambda sy, fc=fc: sy.dma_start(
                    out=accd_o[fc * 128:(fc + 1) * 128, :], in_=acc[:, fc, :]), st, 16)
        bA.free_sem = bB.free_sem = None
        stats2 = emit_ln_stats(p, sems, lambda fc: acc[:, fc, :], TLOC, ones,
                               mb, rb, bA, bB, sq, eps_t)
        p.wait("vector", sc, stats2)
        xres_free = [None, None]
        for fc in range(NFC):
            for tc in range(NTC):
                t0 = tc * 512
                par = (fc * NTC + tc) % 2
                if xres_free[par] is not None:
                    p.wait("sync", vx, xres_free[par])
                lx = p.op("sync", lambda sy, fc=fc, t0=t0, par=par: sy.dma_start(
                    out=xres_sb[:, par, :], in_=x3[fc * 128:fc * 128 + 128, t0:t0 + 512]), ld, 16)
                p.wait("vector", ld, lx)
                p.edged("vector", lambda v, fc=fc, t0=t0: v.tensor_tensor(
                    tmp[:, 0, :], acc[:, fc, t0:t0 + 512], mb[:, t0:t0 + 512],
                    ALU.subtract), vx)
                p.edged("vector", lambda v, t0=t0: v.tensor_tensor(
                    tmp[:, 0, :], tmp[:, 0, :], rb[:, t0:t0 + 512], ALU.mult), vx)
                opar = par
                ov = p.edged("vector", lambda v, par=par, opar=opar: v.tensor_tensor(
                    outst[:, opar, :], tmp[:, 0, :], xres_sb[:, par, :], ALU.add), vx)
                xres_free[par] = ov
                p.wait("sync", vx, ov)
                sv3 = p.op("sync", lambda sy, fc=fc, t0=t0, opar=opar: sy.dma_start(
                    out=out_o[fc * 128:fc * 128 + 128, t0:t0 + 512],
                    in_=outst[:, opar, :]), st, 16)
                p.wait("vector", st, sv3)
        p.wait("sync", st, st.n)
        p.flush(blk)
    return nc



def build_qkv1():
    nc = bass.Bass("TRN2", target_bir_lowering=False, debug=False)
    x2 = nc.dram_tensor("x2", [D, TLOC], BF16, kind="ExternalInput").ap()
    w_attn = nc.dram_tensor("w_attn", [D, 3 * D], F32, kind="ExternalInput").ap()
    b_qk = nc.dram_tensor("b_qk", [128, 2 * NFC], F32, kind="ExternalInput").ap()
    bv_row = nc.dram_tensor("bv_row", [1, D], F32, kind="ExternalInput").ap()
    qT_o = nc.dram_tensor("qT", [D, TLOC], BF16, kind="ExternalOutput").ap()
    kT_o = nc.dram_tensor("kT", [D, TLOC], BF16, kind="ExternalOutput").ap()
    vloc_o = nc.dram_tensor("vloc", [NT, NFC, 128, 128], BF16, kind="ExternalOutput").ap()

    import contextlib
    with contextlib.ExitStack() as _st:
        ones = _st.enter_context(nc.sbuf_tensor("ones", [128, 128], BF16))
        ident = _st.enter_context(nc.sbuf_tensor("ident", [128, 128], BF16))
        eps_t = _st.enter_context(nc.sbuf_tensor("eps_t", [128, 1], F32))
        xln = _st.enter_context(nc.sbuf_tensor("xln", [128, NFC, TLOC], BF16))
        wexp = _st.enter_context(nc.sbuf_tensor("wexp", [128, NFC, D], BF16))
        mb = _st.enter_context(nc.sbuf_tensor("mb", [128, TLOC], F32))
        rb = _st.enter_context(nc.sbuf_tensor("rb", [128, TLOC], F32))
        sq = _st.enter_context(nc.sbuf_tensor("sq", [128, 2, 512], BF16))
        b_sb = _st.enter_context(nc.sbuf_tensor("b_sb", [128, 2 * NFC], F32))
        bvr = _st.enter_context(nc.sbuf_tensor("bvr", [1, D], BF16))
        bvb = _st.enter_context(nc.sbuf_tensor("bvb", [128, D], F32))
        stage = _st.enter_context(nc.sbuf_tensor("stage", [128, 2, 512], BF16))
        tmp = _st.enter_context(nc.sbuf_tensor("tmp", [128, 2, 512], BF16))
        pA = _st.enter_context(nc.psum_tensor("pA", [128, 512], F32))
        pB = _st.enter_context(nc.psum_tensor("pB", [128, 512], F32))
        pp0 = _st.enter_context(nc.psum_tensor("pp0", [128, 512], F32))
        pp1 = _st.enter_context(nc.psum_tensor("pp1", [128, 512], F32))
        blk = _st.enter_context(nc.Block())

        p = Prog(nc)
        ld = p.sem("s_ld"); gld = p.sem("s_gld"); mm = p.sem("s_mm")
        sc = p.sem("s_sc"); vx = p.sem("s_vx"); st = p.sem("s_st")
        gc = p.sem("s_gc")
        sems = dict(mm=mm, sc=sc, vx=vx)
        bA = Bank(pA); bB = Bank(pB)
        bP = [Bank(pp0), Bank(pp1)]

        consts_done = emit_consts(p, gc, ones, ident, eps_t=eps_t)

        for fc in range(NFC):
            p.op("sync", lambda sy, fc=fc: sy.dma_start(
                out=xln[:, fc, :], in_=x2[fc * 128:(fc + 1) * 128, :]), ld, 16)
        p.op("sync", lambda sy: sy.dma_start(out=b_sb[:, :], in_=b_qk), ld, 16)
        b_ld = ld.n
        bld = p.op("gpsimd", lambda g: g.dma_start(out=bvr[:, :], in_=bv_row), gld, 16)

        p.wait("tensor", gld, bld)
        p.wait("tensor", gc, consts_done)
        for c4 in range(4):
            bA.wait_free(p)
            mv = p.op("tensor", lambda t, c4=c4: t.matmul(
                pA[:, :], ones[0:1, 0:128], bvr[0:1, c4 * 512:(c4 + 1) * 512],
                start=True, stop=True), mm)
            p.wait("vector", mm, mv)
            cv = p.edged("vector", lambda v, c4=c4: v.tensor_copy(
                bvb[:, c4 * 512:(c4 + 1) * 512], pA[:, :]), vx)
            bA.set_free(vx, cv)

        p.wait("scalar", ld, 16 * NFC)
        p.wait("tensor", ld, 16 * NFC)
        stats_ready = emit_ln_stats(
            p, sems, lambda fc: xln[:, fc, :], TLOC, ones, mb, rb, bA, bB,
            sq, eps_t)
        xln_ready = emit_normalize_inplace(
            p, sems, xln, mb, rb, tmp[:, 0, :], stats_ready,
            [(mm, mm.n), (sc, sc.n)])

        stage_free = [None, None]

        def qk_evac(out_ap, boff):
            def evac(fc, tc, bank, mv):
                t0 = tc * 512
                spar = (fc * NTC + tc) % 2
                p.wait("scalar", mm, mv)
                p.wait("scalar", ld, b_ld)
                if stage_free[spar] is not None:
                    p.wait("scalar", st, stage_free[spar])
                ev = p.op("scalar", lambda s, fc=fc, bank=bank, spar=spar: s.activation(
                    stage[:, spar, :], bank.tile[:, :], AF.Identity,
                    bias=b_sb[:, boff + fc:boff + fc + 1]), sc)
                bank.set_free(sc, ev)
                p.wait("sync", sc, ev)
                dv = p.op("sync", lambda sy, fc=fc, t0=t0, spar=spar: sy.dma_start(
                    out=out_ap[fc * 128:(fc + 1) * 128, t0:t0 + 512],
                    in_=stage[:, spar, :]), st, 16)
                stage_free[spar] = dv
            return evac

        def load_w(col0):
            p.wait("gpsimd", mm, mm.n)
            wl = None
            for hc in range(NFC):
                wl = p.op("gpsimd", lambda g, hc=hc, col0=col0: g.dma_start(
                    out=wexp[:, hc, :],
                    in_=w_attn[hc * 128:(hc + 1) * 128, col0:col0 + D]), gld, 16)
            return wl

        wl = load_w(0)
        p.wait("tensor", gld, wl)
        emit_featproj(p, sems, bP, xln,
                      lambda hc, fc: wexp[:, hc, fc * 128:(fc + 1) * 128],
                      NFC, stage, qk_evac(qT_o, 0), xln_ready)
        wl = load_w(D)
        stage_free = [None, None]
        p.wait("tensor", gld, wl)
        emit_featproj(p, sems, bP, xln,
                      lambda hc, fc: wexp[:, hc, fc * 128:(fc + 1) * 128],
                      NFC, stage, qk_evac(kT_o, NFC), xln_ready)
        wl = load_w(2 * D)
        stage_free = [None, None]
        p.wait("tensor", gld, wl)

        def v_evac(tt, fq, bank, mv):
            spar = (tt * 4 + fq) % 2
            p.wait("vector", mm, mv)
            if stage_free[spar] is not None:
                p.wait("vector", st, stage_free[spar])
            ev = p.edged("vector", lambda v, fq=fq, bank=bank, spar=spar: v.tensor_tensor(
                stage[:, spar, :], bank.tile[:, :],
                bvb[:, fq * 512:(fq + 1) * 512], ALU.add), vx)
            bank.set_free(vx, ev)
            p.wait("sync", vx, ev)
            dv = None
            for k in range(4):
                fc = fq * 4 + k
                dv = p.op("sync", lambda sy, tt=tt, fc=fc, k=k, spar=spar: sy.dma_start(
                    out=vloc_o[tt, fc, :, :], in_=stage[:, spar, k * 128:(k + 1) * 128]), st, 16)
            stage_free[spar] = dv

        emit_tokproj(p, sems, bP, xln,
                     lambda hc, fq: wexp[:, hc, fq * 512:(fq + 1) * 512],
                     4, v_evac, xln_ready)

        p.wait("sync", st, st.n)
        p.flush(blk)
    return nc



def _rank_rows(j):
    idx = np.empty(TLOC, np.int64)
    for t in range(NT):
        g = 4 * t + j
        idx[t * 128:(t + 1) * 128] = np.arange(g * 128, (g + 1) * 128)
    return idx

_ROWS = [_rank_rows(j) for j in range(4)]


def _col_layout(v, n):
    return np.ascontiguousarray(np.asarray(v, np.float32).reshape(n, 128).T)


def _run(nc, in_maps):
    res = run_bass_kernel_spmd(nc, in_maps, core_ids=list(range(NCORE)))
    return res.results


def kernel(**inp):
    x0 = np.asarray(inp["L0_attn_x"])
    BF_ = BF

    nc = build_qkv0()
    maps = []
    for c in range(NCORE):
        b, j = c // 4, c % 4
        maps.append({
            "x0": np.ascontiguousarray(x0[b][_ROWS[j]].T),
            "wq_codes": inp["L0_attn_wq_codes"], "wq_scales": inp["L0_attn_wq_scales"],
            "wk_codes": inp["L0_attn_wk_codes"], "wk_scales": inp["L0_attn_wk_scales"],
            "wv_codes": inp["L0_attn_wv_codes"], "wv_scales": inp["L0_attn_wv_scales"],
            "bq": _col_layout(inp["L0_attn_bq"], NFC),
            "bk": _col_layout(inp["L0_attn_bk"], NFC),
            "bv_row": np.asarray(inp["L0_attn_bv"], np.float32).reshape(1, D),
        })
    r = _run(nc, maps)
    xres = [r[c]["xres"] for c in range(NCORE)]
    qT = [r[c]["qT"] for c in range(NCORE)]
    kT = [r[c]["kT"] for c in range(NCORE)]
    vloc = [r[c]["vloc"] for c in range(NCORE)]

    def gather_kv(kT, vloc):
        kf, vf = [], []
        for c in range(NCORE):
            b = c // 4
            kf.append(np.stack([kT[4 * b + r_] for r_ in range(4)]))
            vf.append(np.stack([vloc[4 * b + r_] for r_ in range(4)]))
        return kf, vf

    kf, vf = gather_kv(kT, vloc)

    nc = build_attn0()
    maps = [{"qT": qT[c], "kfull": kf[c], "vfull": vf[c]} for c in range(NCORE)]
    r = _run(nc, maps)
    aoT = [r[c]["aoT"] for c in range(NCORE)]

    nc = build_proj(True)
    maps = [{"aoT": aoT[c], "xres": xres[c],
             "w_codes": inp["L0_attn_wo_codes"], "w_scales": inp["L0_attn_wo_scales"],
             "b": _col_layout(inp["L0_attn_bo"], NFC)} for c in range(NCORE)]
    r = _run(nc, maps)
    x1 = [r[c]["xout"] for c in range(NCORE)]

    nc = build_ffn0()
    maps = [{"x1": x1[c],
             "gate_w": inp["L0_ffn_gate_w"], "up_w": inp["L0_ffn_up_w"],
             "down_w": inp["L0_ffn_down_w"],
             "gate_s": _col_layout(inp["L0_ffn_gate_s"], NIC),
             "up_s": _col_layout(inp["L0_ffn_up_s"], NIC),
             "down_s": _col_layout(inp["L0_ffn_down_s"], NFC)} for c in range(NCORE)]
    r = _run(nc, maps)
    x2 = [r[c]["x2"] for c in range(NCORE)]

    nc = build_qkv1()
    b_attn = np.asarray(inp["L1_attn_b_attn"], np.float32)
    b_qk = np.concatenate([_col_layout(b_attn[0:D], NFC),
                           _col_layout(b_attn[D:2 * D], NFC)], axis=1)
    maps = [{"x2": x2[c], "w_attn": inp["L1_attn_w_attn"],
             "b_qk": np.ascontiguousarray(b_qk),
             "bv_row": b_attn[2 * D:].reshape(1, D)} for c in range(NCORE)]
    r = _run(nc, maps)
    qT = [r[c]["qT"] for c in range(NCORE)]
    kT = [r[c]["kT"] for c in range(NCORE)]
    vloc = [r[c]["vloc"] for c in range(NCORE)]
    kf, vf = gather_kv(kT, vloc)

    nc = build_attn1()
    tri = np.triu(np.ones((128, 128), np.float32))
    onem = np.ones((128, 128), np.float32)
    zerm = np.zeros((128, 128), np.float32)
    maps = []
    for c in range(NCORE):
        j = c % 4
        masks = np.stack([onem if r_ < j else (tri if r_ == j else zerm)
                          for r_ in range(4)]).astype(BF_)
        maps.append({"qT": qT[c], "kfull": kf[c], "vfull": vf[c], "masks": masks})
    r = _run(nc, maps)
    aoT = [r[c]["aoT"] for c in range(NCORE)]

    nc = build_proj(False)
    maps = [{"aoT": aoT[c], "xres": x2[c], "w": inp["L1_attn_w_proj"],
             "b": _col_layout(inp["L1_attn_b_proj"], NFC)} for c in range(NCORE)]
    r = _run(nc, maps)
    x3 = [r[c]["xout"] for c in range(NCORE)]

    nc = build_moe()
    maps = [{"x3": x3[c], "gate_w": inp["L1_ffn_gate_w"],
             "Wg": inp["L1_ffn_Wg"], "Wu": inp["L1_ffn_Wu"], "Wd": inp["L1_ffn_Wd"],
             "sWg": inp["L1_ffn_sWg"], "sWu": inp["L1_ffn_sWu"],
             "sWd": inp["L1_ffn_sWd"]} for c in range(NCORE)]
    r = _run(nc, maps)

    out = np.empty((B, S, D), np.float32)
    for c in range(NCORE):
        b, j = c // 4, c % 4
        out[b][_ROWS[j]] = r[c]["out"].T
    return out
